# version 03
# baseline (speedup 1.0000x reference)
import numpy as np
import ml_dtypes
import concourse.bass as bass
import concourse.mybir as mybir
from concourse.bass_core_rust import IndirectOffsetOnAxis

F32 = mybir.dt.float32
BF16 = mybir.dt.bfloat16
I32 = mybir.dt.int32
AF = mybir.ActivationFunctionType
ALU = mybir.AluOpType
AX = mybir.AxisListType
BF = ml_dtypes.bfloat16

BODY_OF_QUEUE = {"q_sync": "sync", "q_scalar": "scalar", "q_gpsimd": "gpsimd"}


class Stream:
    def __init__(self, name, inc_unit, body):
        self.name = name
        self.inc_unit = inc_unit
        self.body = body
        self.sem = None
        self.n_ops = 0


class Op:
    __slots__ = ("stream", "emitf", "deps", "inc_flag", "cum", "hazard",
                 "force_same", "tag")

    def __init__(self, stream, emitf, tag=""):
        self.stream = stream
        self.emitf = emitf
        self.deps = []
        self.inc_flag = False
        self.cum = 0
        self.hazard = False
        self.force_same = False
        self.tag = tag


class Buf:
    __slots__ = ("name", "writers", "readers")

    def __init__(self, name):
        self.name = name
        self.writers = {}
        self.readers = []


class ST:

    def __init__(self, h, buf):
        self.h = h
        self.b = buf

    def __getitem__(self, key):
        return self.h[key]

    def ap(self):
        return self.h.ap()


class Prog:
    def __init__(self, name="prog"):
        self.nc = bass.Bass("TRN2", target_bir_lowering=False, name=name)
        self.streams = {}
        for e in ("tensor", "vector", "scalar", "gpsimd"):
            self.streams[e] = Stream(e, 1, e)
        self.bodies = {e: [] for e in ("sync", "tensor", "vector", "scalar", "gpsimd")}
        self.extra_inputs = {}
        self._registered_imms = set()
        self._cimm_vals = []
        self._cimm_t = self.sbuf("cimm", [128, 32], F32)
        self._finalized = False
        self._n_ops = 0
        self._chains = {}

    def sbuf(self, name, shape, dtype):
        h = self.nc.alloc_sbuf_tensor(name, shape, dtype)
        return ST(h, Buf(name))

    def psum(self, name, shape, dtype=F32):
        h = self.nc.alloc_psum_tensor(name, shape, dtype)
        return ST(h, Buf(name))

    def arena(self, nbytes):
        r = self.nc.bump_sbuf(nbytes)
        assert r is not None, f"arena alloc failed ({nbytes} B/partition)"
        return r[0]

    def at(self, offset, name, shape, dtype):
        h = self.nc.alloc_sbuf_tensor_at(name, shape, dtype, offset=offset)
        return ST(h, Buf(name))

    def sbuf_group(self, name, n, shape, dtype):
        itemsz = {F32: 4, BF16: 2, I32: 4}[dtype]
        parent = self.sbuf(name, [shape[0], shape[1] * n], dtype)
        off = self.nc.lookup_mloc(parent.h).addr
        return [ST(self.nc.alloc_sbuf_tensor_at(f"{name}_{k}", shape, dtype,
                                                offset=off + k * shape[1] * itemsz),
                   parent.b) for k in range(n)]

    def at_group(self, offset, name, n, shape, dtype):
        itemsz = {F32: 4, BF16: 2, I32: 4}[dtype]
        b = Buf(name)
        return [ST(self.nc.alloc_sbuf_tensor_at(f"{name}_{k}", shape, dtype,
                                                offset=offset + k * shape[1] * itemsz),
                   b) for k in range(n)]

    def claim(self, new_sts, old_sts):
        marks = []
        for o in old_sts:
            b = o.b if isinstance(o, ST) else o
            marks.extend(b.writers.values())
            marks.extend(b.readers)
        for nw in new_sts:
            b = nw.b if isinstance(nw, ST) else nw
            b.readers = list(dict.fromkeys(b.readers + marks))

    def dram(self, name, shape, dtype):
        h = self.nc.dram_tensor(name, shape, dtype, kind="Internal")
        return ST(h, Buf(name))

    def din(self, name, shape, dtype):
        h = self.nc.dram_tensor(name, shape, dtype, kind="ExternalInput")
        return ST(h, Buf(name))

    def dout(self, name, shape, dtype):
        h = self.nc.dram_tensor(name, shape, dtype, kind="ExternalOutput")
        return ST(h, Buf(name))

    def cin(self, name, arr):
        arr = np.ascontiguousarray(arr)
        dt = {np.dtype(np.float32): F32, np.dtype(np.int32): I32,
              np.dtype(BF): BF16}[arr.dtype]
        self.extra_inputs[name] = arr
        h = self.nc.dram_tensor(name, list(arr.shape), dt, kind="ExternalInput")
        return ST(h, Buf(name))

    def imm(self, v):
        v = float(v)
        if v not in self._registered_imms:
            idx = len(self._cimm_vals)
            assert idx < 32
            self._cimm_vals.append(v)
            self.nc.const_aps.register(F32, v, self._cimm_t.h[:, idx:idx + 1])
            self._registered_imms.add(v)
        return v

    def _qstream(self, body, writes, reads=()):
        assert len(writes) >= 1, "dma op needs a written buf"
        b0 = writes[0]
        bname = (b0.b if isinstance(b0, ST) else b0).name
        key = f"q_{body}@{bname}"
        if reads:
            r0 = reads[0]
            key += "@" + (r0.b if isinstance(r0, ST) else r0).name
        if key not in self.streams:
            self.streams[key] = Stream(key, 16, body)
        return key

    def _op(self, body, stream_name, emitf, reads=(), writes=(), tag="",
            hazard=False, force_same=False):
        st = self.streams[stream_name]
        o = Op(st, emitf, tag)
        o.hazard = hazard
        o.force_same = force_same
        deps = []
        for t in reads:
            b = t.b if isinstance(t, ST) else t
            deps.extend(b.writers.values())
            b.readers.append(o)
        for t in writes:
            b = t.b if isinstance(t, ST) else t
            deps.extend(b.writers.values())
            deps.extend(b.readers)
            b.writers = {st.name: o}
            b.readers = []
        o.deps = deps
        self.bodies[body].append(o)
        self._n_ops += 1
        return o

    def dma(self, body, out, in_, reads=(), writes=(), tag="dma", chain=None,
            **kw):
        if chain is not None:
            q = f"q_{body}@{chain}"
            if q not in self.streams:
                self.streams[q] = Stream(q, 16, body)
            o = self._op(body, q, lambda e: e.dma_start(out=out, in_=in_, **kw),
                         reads=reads, writes=writes, tag=tag, force_same=True)
            ch = self._chains.setdefault(q, [])
            if ch:
                o.deps.append(ch[-1])
            ch.append(o)
            return o
        q = self._qstream(body, writes, reads)
        return self._op(body, q,
                        lambda e: e.dma_start(out=out, in_=in_, **kw),
                        reads=reads, writes=writes, tag=tag)

    def dmaT(self, body, out, in_, reads=(), writes=(), tag="dmaT"):
        assert body in ("sync", "scalar")
        q = self._qstream(body, writes)
        return self._op(body, q,
                        lambda e: e.dma_start(out=out, in_=in_, transpose=True),
                        reads=reads, writes=writes, tag=tag)

    def indirect(self, out, out_off, in_, in_off, reads=(), writes=(),
                 bounds_check=None, tag="ind", **kw):
        return self._op("gpsimd", self._qstream("gpsimd", writes),
                        lambda e: e.indirect_dma_start(
                            out, out_off, in_, in_off,
                            bounds_check=bounds_check, oob_is_err=False, **kw),
                        reads=reads, writes=writes, tag=tag)

    def mm(self, out, lhsT, rhs, reads=(), writes=(), start=True, stop=True,
           tile_position=None, tag="mm"):
        return self._op("tensor", "tensor",
                        lambda e: e.matmul(out, lhsT, rhs, start=start, stop=stop,
                                           tile_position=tile_position),
                        reads=reads, writes=writes, tag=tag)

    def transpose(self, out, in_, ident, reads=(), writes=(), tag="tr"):
        return self._op("tensor", "tensor",
                        lambda e: e.matmul(out, in_, ident, start=True,
                                           stop=True, is_transpose=True),
                        reads=reads, writes=writes, tag=tag)

    def act(self, out, in_, func, bias=0.0, scale=1.0, accum_out=None,
            reads=(), writes=(), tag="act"):
        reads = list(reads)
        if isinstance(bias, float):
            bias = self.imm(bias)
            reads.append(self._cimm_t)
        if isinstance(scale, float):
            scale = self.imm(scale)
            reads.append(self._cimm_t)
        return self._op("scalar", "scalar",
                        lambda e: e.activation(out, in_, func, bias=bias,
                                               scale=scale, accum_out=accum_out),
                        reads=reads, writes=writes, tag=tag)

    def vop(self, emitf, reads=(), writes=(), tag="v", hazard=False,
            force_same=False, engine="vector"):
        return self._op(engine, engine, emitf, reads=reads, writes=writes,
                        tag=tag, hazard=hazard, force_same=force_same)

    def copy(self, out, in_, reads=(), writes=(), tag="cp", engine="vector"):
        if engine == "scalar":
            return self._op(engine, engine, lambda e: e.copy(out, in_),
                            reads=reads, writes=writes, tag=tag)
        return self._op(engine, engine, lambda e: e.tensor_copy(out, in_),
                        reads=reads, writes=writes, tag=tag)

    def tt(self, out, a, b, op, reads=(), writes=(), tag="tt", engine="vector"):
        return self._op(engine, engine, lambda e: e.tensor_tensor(out, a, b, op),
                        reads=reads, writes=writes, tag=tag)

    def ts(self, out, in0, s1, s2=None, op0=ALU.mult, op1=ALU.bypass,
           reads=(), writes=(), tag="ts", engine="vector"):
        if s2 is None:
            return self._op(engine, engine,
                            lambda e: e.tensor_scalar(out, in0, s1, None, op0),
                            reads=reads, writes=writes, tag=tag)
        return self._op(engine, engine,
                        lambda e: e.tensor_scalar(out, in0, s1, s2, op0, op1),
                        reads=reads, writes=writes, tag=tag)

    def stt(self, out, in0, scalar, in1, op0, op1, reads=(), writes=(), tag="stt",
            engine="vector"):
        return self._op(engine, engine,
                        lambda e: e.scalar_tensor_tensor(out, in0, scalar, in1, op0, op1),
                        reads=reads, writes=writes, tag=tag)

    def reduce(self, out, in_, op, reads=(), writes=(), tag="red", engine="vector"):
        return self._op(engine, engine,
                        lambda e: e.tensor_reduce(out, in_, AX.X, op),
                        reads=reads, writes=writes, tag=tag)

    def scan(self, out, d0, d1, initial, op0, op1, reads=(), writes=(), tag="scan"):
        return self._op("vector", "vector",
                        lambda e: e.tensor_tensor_scan(out, d0, d1, initial, op0, op1),
                        reads=reads, writes=writes, tag=tag, force_same=True)

    def memset(self, ap, v, writes=(), tag="memset", engine="vector"):
        return self._op(engine, engine, lambda e: e.memset(ap, v),
                        writes=writes, tag=tag, hazard=True)

    def recip(self, out, in_, reads=(), writes=(), tag="recip"):
        return self._op("vector", "vector", lambda e: e.reciprocal(out, in_),
                        reads=reads, writes=writes, tag=tag)

    def select(self, out, mask, t, f, reads=(), writes=(), tag="sel"):
        return self._op("vector", "vector", lambda e: e.select(out, mask, t, f, add_drain=True),
                        reads=reads, writes=writes, tag=tag)

    def finalize(self):
        assert not self._finalized
        self._finalized = True
        cimm = np.zeros((128, 32), np.float32)
        for i, v in enumerate(self._cimm_vals):
            cimm[:, i] = v
        self.extra_inputs["cimm_src"] = cimm
        cimm_h = self.nc.dram_tensor("cimm_src", [128, 32], F32, kind="ExternalInput")
        key = self._qstream("sync", [self._cimm_t])
        ld = Op(self.streams[key],
                lambda e: e.dma_start(out=self._cimm_t.h[:], in_=cimm_h.ap()))
        for o in self._cimm_t.b.readers:
            o.deps.append(ld)
        self.bodies["sync"].insert(0, ld)

        for ename, ops in self.bodies.items():
            for o in ops:
                if o.stream.inc_unit == 16:
                    o.inc_flag = True
        for ename, ops in self.bodies.items():
            for o in ops:
                kept = []
                seen = set()
                for d in o.deps:
                    if d is o:
                        continue
                    if (d.stream is o.stream
                            and (d.stream.name == "tensor"
                                 or d.stream.inc_unit == 16)
                            and not (d.hazard or o.force_same)):
                        continue
                    if id(d) in seen:
                        continue
                    seen.add(id(d))
                    d.inc_flag = True
                    kept.append(d)
                o.deps = kept
        n_sems = 0
        for sname, st in self.streams.items():
            cum = 0
            for o in self.bodies[st.body]:
                if o.stream is st:
                    if o.inc_flag:
                        cum += st.inc_unit
                    o.cum = cum
            st.total = cum
            if cum > 0:
                st.sem = self.nc.alloc_semaphore(sname.replace('@', '_'))
                n_sems += 1
        assert n_sems <= 95, f"too many semaphores: {n_sems}"
        self.n_sems = n_sems

        nc = self.nc
        qstreams = [st for st in self.streams.values()
                    if st.inc_unit == 16 and st.total > 0]
        with nc.Block() as blk:
            for ename in ("sync", "vector", "scalar", "tensor", "gpsimd"):
                ops = self.bodies[ename]
                final = ename == "gpsimd"

                def make(ops=ops, final=final):
                    def body(eng):
                        hw = {}
                        for o in ops:
                            for d in o.deps:
                                k = d.stream.name
                                if hw.get(k, 0) < d.cum:
                                    eng.wait_ge(d.stream.sem, d.cum)
                                    hw[k] = d.cum
                            ins = o.emitf(eng)
                            if o.inc_flag:
                                ins.then_inc(o.stream.sem, o.stream.inc_unit)
                        if final:
                            for st in qstreams:
                                if hw.get(st.name, 0) < st.total:
                                    eng.wait_ge(st.sem, st.total)
                    return body

                getattr(blk, ename)(make())
        return nc

    def barrier_reads(self, engine, ts):
        return self._op(engine, engine, lambda e: e.nop(), reads=ts, tag="barrier")


def bf16c(x):
    return np.ascontiguousarray(x.astype(BF))


"""Block emitters for the mk_prb_d2 megakernel (one core's slice, batch-sharded).

Layout conventions:
 - residual stream x lives in DRAM token-major [T, 1024] f32
 - on-chip activations for matmuls are "transposed": [channel on partitions,
   tokens on free]; produced via DRAM round-trip + dma_start_transpose
 - matmul out = lhsT.T @ rhs, K on partitions of both operands
"""

D = 1024
HID = 4096
NH = 16
KD = 16
VD = 64


def common_consts(p):
    c = {}
    c["ident_f32"] = p.cin("c_ident_f32", np.eye(128, dtype=np.float32))
    c["ident_bf"] = p.cin("c_ident_bf", np.eye(128, dtype=BF))
    c["ones1"] = p.cin("c_ones1", np.ones((1, 128), np.float32))
    si = {}
    for nm, t in c.items():
        s = p.sbuf("s_" + nm, list(t.h.shape), t.h.dtype)
        p.dma("sync", s[:], t.ap(), reads=[t], writes=[s], chain="constld")
        si[nm] = s
    return si


def load_scale_col(p, consts, dram_vec, n, name, ps):
    row = p.sbuf(name + "_row", [n, 128], F32)
    col = p.sbuf(name, [128, n], F32)
    p.dma("sync", row[:], dram_vec.ap().rearrange("(m q) -> m q", q=128),
          reads=[dram_vec], writes=[row])
    p.transpose(ps.h[0:128, 0:n], row[:], consts["ident_f32"][0:n, 0:n],
                reads=[row, consts["ident_f32"]], writes=[ps])
    p.copy(col[:], ps.h[0:128, 0:n], reads=[ps], writes=[col], engine="scalar")
    return col


def bcast_row(p, consts, row_st, row_ap, dst, width, ps):
    for c0 in range(0, width, 512):
        w = min(512, width - c0)
        p.mm(ps[0:128, 0:w], consts["ones1"][:], row_ap[0:1, c0:c0 + w],
             reads=[consts["ones1"], row_st], writes=[ps])
        p.copy(dst[:, c0:c0 + w], ps[0:128, 0:w], reads=[ps], writes=[dst],
               engine="scalar")


def emit_ln_plain(p, x_dram, T, xn_bf_dram, bufs, pfx="ln"):
    nt = T // 128
    sx, ssq, sxn, st_ = bufs
    for tb in range(nt):
        i = tb % 2
        x_t, sq_t, xn_t, s = sx[i], ssq[i], sxn[i], st_[i]
        p.dma("sync", x_t[:], x_dram.ap()[tb * 128:(tb + 1) * 128, :],
              reads=[x_dram], writes=[x_t])
        mu, m2, mu2, var, rv, rstd = (s[:, 0:1], s[:, 1:2], s[:, 2:3],
                                      s[:, 3:4], s[:, 4:5], s[:, 5:6])
        p.reduce(mu, x_t[:], ALU.add, reads=[x_t], writes=[s])
        p.act(sq_t[:], x_t[:], AF.Square, reads=[x_t], writes=[sq_t])
        p.reduce(m2, sq_t[:], ALU.add, reads=[sq_t], writes=[s])
        p.ts(mu, mu, 1.0 / D, reads=[s], writes=[s])
        p.ts(m2, m2, 1.0 / D, reads=[s], writes=[s])
        p.tt(mu2, mu, mu, ALU.mult, reads=[s], writes=[s])
        p.tt(var, m2, mu2, ALU.subtract, reads=[s], writes=[s])
        p.ts(var, var, 1e-5, None, ALU.add, reads=[s], writes=[s])
        p.recip(rv, var, reads=[s], writes=[s])
        p.act(rstd, rv, AF.Sqrt, reads=[s], writes=[s])
        p.ts(xn_t[:], x_t[:], mu, rstd, ALU.subtract, ALU.mult,
             reads=[x_t, s], writes=[xn_t])
        p.dma("sync", xn_bf_dram.ap()[tb * 128:(tb + 1) * 128, :], xn_t[:],
              reads=[xn_t], writes=[xn_bf_dram])


def load_T_slabs(p, dram_bf, rows, cols, name, tiles=None):
    if tiles is None:
        tiles = p.sbuf_group(name, cols // 128, [128, rows], BF16)
    for k in range(cols // 128):
        p.dmaT("sync", tiles[k][:], dram_bf.ap()[0:rows, k * 128:(k + 1) * 128],
               reads=[dram_bf], writes=[tiles[k]])
    return tiles


def emit_l0(p, io, T=4096):
    consts = common_consts(p)
    nt = T // 128
    nnc = T // 512
    ncc = max(1, T // 1024)
    ccw = min(1024, T)

    w1_bf = p.dram("w1_bf", [HID, D], BF16)
    w2_bf = p.dram("w2_bf", [HID, D], BF16)
    h1_bf = p.dram("h1_bf", [HID, HID], BF16)
    h2_bf = p.dram("h2_bf", [HID, HID], BF16)
    w3_bf = p.dram("w3_bf", [D, HID], BF16)
    qp_bf = p.dram("qp_bf", [NH * KD, D], BF16)
    kp_bf = p.dram("kp_bf", [NH * KD, D], BF16)
    vp_bf = p.dram("vp_bf", [NH * VD, D], BF16)
    op_bf = p.dram("op_bf", [D, NH * VD], BF16)
    p.dma("gpsimd", qp_bf.ap().rearrange("(h k) d -> h k d", k=KD),
          io["qp"].ap().transpose([1, 0, 2]),
          reads=[io["qp"]], writes=[qp_bf], tag="cast", chain="casts")
    p.dma("gpsimd", kp_bf.ap(), io["kp"].ap().rearrange("h k d -> (h k) d"),
          reads=[io["kp"]], writes=[kp_bf], tag="cast", chain="casts")
    p.dma("gpsimd", vp_bf.ap(), io["vp"].ap().rearrange("h v d -> (h v) d"),
          reads=[io["vp"]], writes=[vp_bf], tag="cast", chain="casts")
    p.dma("gpsimd", op_bf.ap(), io["op"].ap().rearrange("d h v -> d (h v)"),
          reads=[io["op"]], writes=[op_bf], tag="cast", chain="casts")
    for s, d in (("w1_w", w1_bf), ("w2_w", w2_bf), ("h1_w", h1_bf),
                 ("h2_w", h2_bf), ("w3_w", w3_bf)):
        p.dma("gpsimd", d.ap(), io[s].ap(), reads=[io[s]], writes=[d], tag="cast", chain="casts")

    ps2 = [p.psum(f"ps2_{i}", [128, 1024], F32) for i in range(2)]
    pb = [p.psum(f"psb_{i}", [128, 512], F32) for i in range(4)]

    s1c = load_scale_col(p, consts, io["w1_s"], 32, "s1c", pb[0])
    s2c = load_scale_col(p, consts, io["w2_s"], 32, "s2c", pb[1])
    h1c = load_scale_col(p, consts, io["h1_s"], 32, "h1c", pb[2])
    h2c = load_scale_col(p, consts, io["h2_s"], 32, "h2c", pb[3])
    s3row = p.sbuf("s3row", [1, D], F32)
    p.dma("sync", s3row[:], io["w3_s"].ap().rearrange("(o d) -> o d", o=1),
          reads=[io["w3_s"]], writes=[s3row])
    s3bc = p.sbuf("s3bc", [128, D], F32)
    bcast_row(p, consts, s3row, s3row.h, s3bc, D, pb[0])

    AR = 128 * 1024
    base = p.arena(AR)
    cur = [base]

    def aat(name, shape, dtype):
        st = p.at(cur[0], name, shape, dtype)
        itemsz = {F32: 4, BF16: 2, I32: 4}[dtype]
        cur[0] += shape[1] * itemsz
        assert cur[0] <= base + AR, f"arena overflow at {name}"
        return st

    ln_bufs = ([aat(f"ln_x{i}", [128, D], F32) for i in range(2)],
               [aat(f"ln_sq{i}", [128, D], F32) for i in range(2)],
               [aat(f"ln_o{i}", [128, D], BF16) for i in range(2)],
               [aat(f"ln_st{i}", [128, 8], F32) for i in range(2)])
    ln_all = [t for grp in ln_bufs for t in grp]

    xn_bf = io.get("dbg_xn") or p.dram("xn_bf", [T, D], BF16)
    emit_ln_plain(p, io["x"], T, xn_bf, ln_bufs)

    xnT = load_T_slabs(p, xn_bf, T, D, "xnT")

    cur[0] = base
    wqT = p.at_group(cur[0], "wqT", D // 128, [128, NH * KD], BF16)
    cur[0] += D // 128 * NH * KD * 2
    wkT = p.at_group(cur[0], "wkT", D // 128, [128, NH * KD], BF16)
    cur[0] += D // 128 * NH * KD * 2
    load_T_slabs(p, qp_bf, NH * KD, D, "wqT", tiles=wqT)
    load_T_slabs(p, kp_bf, NH * KD, D, "wkT", tiles=wkT)
    qt = aat("qt_pack", [128, T], BF16)
    kt = aat("kt_pack", [128, T], BF16)
    oacc = [aat(f"oacc{j}", [64, T], F32) for j in range(2)]
    et = [[aat(f"et_{j}_{m}", [128, T], BF16) for m in range(4)] for j in range(2)]

    vps = p.at_group(cur[0], "vps", 8, [128, 128], BF16)
    cur[0] += 8 * 128 * 2
    attn_set = ([qt, kt] + oacc + et[0] + et[1] + vps + wqT + wkT)
    p.claim(attn_set, ln_all)

    zp = [[p.sbuf(f"zp_{j}_{m}", [128, 8], F32) for m in range(4)] for j in range(2)]
    vsc = [[p.sbuf(f"vsc_{j}_{m}", [128, 64], BF16) for m in range(4)] for j in range(2)]
    zst = [[p.sbuf(f"zst_{j}_{m}", [128, 2], F32) for m in range(4)] for j in range(2)]

    ocat = io.get("dbg_ocat") or p.dram("ocat", [D, T], BF16)

    for g in range(NH // 2):
        for (slabs, pack) in ((wqT, qt), (wkT, kt)):
            for c in range(nnc):
                for j in range(2):
                    h = 2 * g + j
                    bank = pb[j]
                    for k in range(8):
                        p.mm(bank[32 * j:32 * j + 16, :],
                             slabs[k][:, h * 16:(h + 1) * 16],
                             xnT[k][:, c * 512:(c + 1) * 512],
                             reads=[slabs[k], xnT[k]], writes=[bank],
                             start=(k == 0), stop=(k == 7),
                             tile_position=(0, 32 * j))
                    p.act(pack[32 * j:32 * j + 16, c * 512:(c + 1) * 512],
                          bank[32 * j:32 * j + 16, :], AF.Copy,
                          reads=[bank], writes=[pack])
        for k in range(8):
            p.dmaT("sync", vps[k][:], vp_bf.ap()[g * 128:(g + 1) * 128,
                                                 k * 128:(k + 1) * 128],
                   reads=[vp_bf], writes=[vps[k]])
        for j in range(2):
            p.memset(oacc[j][:], 0.0, writes=[oacc[j]])
        MS = 4 if nt % 4 == 0 else 2
        for sup in range(nt // MS):
            for m2 in range(MS):
                mb = sup * MS + m2
                for cc in range(ncc):
                    for j in range(2):
                        sps = ps2[j]
                        for half in range(ccw // 512):
                            c0 = cc * ccw + half * 512
                            p.mm(sps[0:128, half * 512:half * 512 + 512],
                                 kt[32 * j:32 * j + 16, mb * 128:(mb + 1) * 128],
                                 qt[32 * j:32 * j + 16, c0:c0 + 512],
                                 reads=[kt, qt], writes=[sps],
                                 start=True, stop=True,
                                 tile_position=(32 * j, 0))
                    for j in range(2):
                        p.act(et[j][m2][:, cc * ccw:(cc + 1) * ccw],
                              ps2[j][0:128, 0:ccw], AF.Exp, scale=0.25,
                              accum_out=zp[j][m2][:, cc:cc + 1],
                              reads=[ps2[j]], writes=[et[j][m2], zp[j][m2]])
            for m2 in range(MS):
                mb = sup * MS + m2
                bank = pb[2 + (m2 % 2)]
                for k in range(8):
                    p.mm(bank[0:128, 0:128], xnT[k][:, mb * 128:(mb + 1) * 128],
                         vps[k][:], reads=[xnT[k], vps[k]], writes=[bank],
                         start=(k == 0), stop=(k == 7))
                for j in range(2):
                    z, rz = zst[j][m2][:, 0:1], zst[j][m2][:, 1:2]
                    p.reduce(z, zp[j][m2][:, 0:ncc], ALU.add,
                             reads=[zp[j][m2]], writes=[zst[j][m2]])
                    p.recip(rz, z, reads=[zst[j][m2]], writes=[zst[j][m2]])
                    p.act(vsc[j][m2][:], bank[0:128, j * 64:(j + 1) * 64],
                          AF.Copy, scale=rz, reads=[bank, zst[j][m2]],
                          writes=[vsc[j][m2]])
            for c in range(nnc):
                for j in range(2):
                    bank = pb[2 + j]
                    for m2 in range(MS):
                        p.mm(bank[0:64, :], vsc[j][m2][:],
                             et[j][m2][:, c * 512:(c + 1) * 512],
                             reads=[vsc[j][m2], et[j][m2]], writes=[bank],
                             start=(m2 == 0), stop=(m2 == MS - 1))
                    p.tt(oacc[j][0:64, c * 512:(c + 1) * 512],
                         oacc[j][0:64, c * 512:(c + 1) * 512], bank[0:64, 0:512],
                         ALU.add, reads=[oacc[j], bank], writes=[oacc[j]])
        for j in range(2):
            p.dma("gpsimd", ocat.ap()[g * 128 + j * 64:g * 128 + (j + 1) * 64, :],
                  oacc[j][0:64, :], reads=[oacc[j]], writes=[ocat])

    fcur = [base]

    def fat(name, shape, dtype):
        st = p.at(fcur[0], name, shape, dtype)
        itemsz = {F32: 4, BF16: 2, I32: 4}[dtype]
        fcur[0] += shape[1] * itemsz
        assert fcur[0] <= base + AR, f"arena overflow at {name}"
        return st

    TCN = T // 512
    h_t = fat("h_t", [128, (HID // 128) * 512], BF16)
    g_t = fat("g_t", [128, (HID // 128) * 512], BF16)
    sl1 = [[fat(f"sl1_{m}_{i}", [128, 512], BF16) for i in range(2)] for m in range(2)]
    asil = [fat(f"asil_{i}", [128, 512], BF16) for i in range(4)]
    btmp = [fat(f"btmp_{i}", [128, 512], BF16) for i in range(4)]
    sl2 = [fat(f"sl2_{i}", [128, 512], BF16) for i in range(2)]
    sw3 = [fat(f"sw3_{i}", [128, 1024], BF16) for i in range(2)]
    soc = [fat(f"soc_{i}", [128, 512], BF16) for i in range(2)]
    sop = [fat(f"sop_{i}", [128, 1024], BF16) for i in range(2)]
    tmp4 = [fat(f"tmp4_{i}", [128, 1024], F32) for i in range(4)]
    x1t = [fat(f"x1t_{i}", [128, 1024], F32) for i in range(4)]
    ffn_set = ([h_t, g_t] + sl1[0] + sl1[1] + asil + btmp + sl2 + sw3 + soc +
               sop + tmp4 + x1t)
    p.claim(ffn_set, attn_set + ln_all)

    banks8 = [ps2[0][0:128, 0:512], ps2[0][0:128, 512:1024],
              ps2[1][0:128, 0:512], ps2[1][0:128, 512:1024],
              pb[0][:], pb[1][:], pb[2][:], pb[3][:]]
    banks8_st = [ps2[0], ps2[0], ps2[1], ps2[1], pb[0], pb[1], pb[2], pb[3]]
    nmo = HID // 128

    for tc in range(TCN):
        t0 = tc * 512
        for mog in range(nmo // 4):
            for mi, (wbf, scol, dst) in enumerate(
                    ((w1_bf, s1c, asil), (w2_bf, s2c, btmp))):
                for k in range(8):
                    sl = sl1[mi][k % 2]
                    p.dmaT("sync", sl[:],
                           wbf.ap()[mog * 512:(mog + 1) * 512, k * 128:(k + 1) * 128],
                           reads=[wbf], writes=[sl])
                    for mo4 in range(4):
                        bk = banks8[4 * mi + mo4]
                        bst = banks8_st[4 * mi + mo4]
                        p.mm(bk, sl[:, mo4 * 128:(mo4 + 1) * 128],
                             xnT[k][:, t0:t0 + 512],
                             reads=[sl, xnT[k]], writes=[bst],
                             start=(k == 0), stop=(k == 7))
                for mo4 in range(4):
                    mo = mog * 4 + mo4
                    fn = AF.Silu if mi == 0 else AF.Copy
                    p.act(dst[mo4][:], banks8[4 * mi + mo4], fn,
                          scale=(s1c if mi == 0 else s2c)[:, mo:mo + 1],
                          reads=[banks8_st[4 * mi + mo4], s1c if mi == 0 else s2c],
                          writes=[dst[mo4]])
            for mo4 in range(4):
                mo = mog * 4 + mo4
                p.tt(h_t[:, mo * 512:(mo + 1) * 512], asil[mo4][:], btmp[mo4][:],
                     ALU.mult, reads=[asil[mo4], btmp[mo4]], writes=[h_t])
        for (hbf, hcol, src_t, dst_t) in ((h1_bf, h1c, h_t, g_t),
                                          (h2_bf, h2c, g_t, h_t)):
            for mog in range(nmo // 8):
                for k in range(nmo):
                    sl = sw3[k % 2]
                    p.dmaT("sync", sl[:],
                           hbf.ap()[mog * 1024:(mog + 1) * 1024,
                                    k * 128:(k + 1) * 128],
                           reads=[hbf], writes=[sl])
                    for mo8 in range(8):
                        p.mm(banks8[mo8], sl[:, mo8 * 128:(mo8 + 1) * 128],
                             src_t[:, k * 512:(k + 1) * 512],
                             reads=[sl, src_t], writes=[banks8_st[mo8]],
                             start=(k == 0), stop=(k == nmo - 1))
                for mo8 in range(8):
                    mo = mog * 8 + mo8
                    p.act(dst_t[:, mo * 512:(mo + 1) * 512], banks8[mo8],
                          AF.Gelu, scale=hcol[:, mo:mo + 1],
                          reads=[banks8_st[mo8], hcol], writes=[dst_t])
        op_banks = [(ps2[0], ps2[0][0:128, 0:512], ps2[0][0:128, 512:1024]),
                    (ps2[1], ps2[1][0:128, 0:512], ps2[1][0:128, 512:1024]),
                    (None, pb[0][:], pb[1][:]),
                    (None, pb[2][:], pb[3][:])]
        for k2 in range(8):
            so_ = soc[k2 % 2]
            p.dma("sync", so_[:], ocat.ap()[k2 * 128:(k2 + 1) * 128, t0:t0 + 512],
                  reads=[ocat], writes=[so_])
            sp = sop[k2 % 2]
            p.dmaT("sync", sp[:], op_bf.ap()[0:D, k2 * 128:(k2 + 1) * 128],
                   reads=[op_bf], writes=[sp])
            for ts4 in range(4):
                stw, bk0, bk1 = op_banks[ts4]
                sts = [x for x in (stw, pb[0] if ts4 == 2 else None,
                                   pb[1] if ts4 == 2 else None,
                                   pb[2] if ts4 == 3 else None,
                                   pb[3] if ts4 == 3 else None) if x is not None]
                for half, bk in enumerate((bk0, bk1)):
                    p.mm(bk, so_[:, ts4 * 128:(ts4 + 1) * 128],
                         sp[:, half * 512:(half + 1) * 512],
                         reads=[so_, sp], writes=sts,
                         start=(k2 == 0), stop=(k2 == 7))
        for ts4 in range(4):
            stw, bk0, bk1 = op_banks[ts4]
            rd = [stw] if stw is not None else [pb[0], pb[1]] if ts4 == 2 else [pb[2], pb[3]]
            p.copy(tmp4[ts4][:, 0:512], bk0, reads=rd, writes=[tmp4[ts4]],
                   engine="scalar")
            p.copy(tmp4[ts4][:, 512:1024], bk1, reads=rd, writes=[tmp4[ts4]],
                   engine="scalar")
            xi = x1t[ts4]
            p.dma("sync", xi[:], io["x"].ap()[t0 + ts4 * 128:t0 + (ts4 + 1) * 128, :],
                  reads=[io["x"]], writes=[xi])
            p.tt(tmp4[ts4][:], tmp4[ts4][:], xi[:], ALU.add,
                 reads=[tmp4[ts4], xi], writes=[tmp4[ts4]])
        w3_banks = [(banks8_st[2 * i], banks8[2 * i], banks8[2 * i + 1])
                    for i in range(4)]
        for k in range(nmo):
            sl = sw3[k % 2]
            p.dmaT("sync", sl[:], w3_bf.ap()[0:D, k * 128:(k + 1) * 128],
                   reads=[w3_bf], writes=[sl])
            for ts4 in range(4):
                stw, bk0, bk1 = w3_banks[ts4]
                sts = [stw] if ts4 < 2 else [banks8_st[2 * ts4], banks8_st[2 * ts4 + 1]]
                for half, bk in enumerate((bk0, bk1)):
                    p.mm(bk, h_t[:, k * 512 + ts4 * 128: k * 512 + (ts4 + 1) * 128],
                         sl[:, half * 512:(half + 1) * 512],
                         reads=[h_t, sl], writes=sts,
                         start=(k == 0), stop=(k == nmo - 1))
        for ts4 in range(4):
            stw, bk0, bk1 = w3_banks[ts4]
            rd = [stw] if ts4 < 2 else [banks8_st[2 * ts4], banks8_st[2 * ts4 + 1]]
            xo = x1t[ts4]
            p.tt(xo[:, 0:512], bk0, s3bc[:, 0:512], ALU.mult,
                 reads=rd + [s3bc], writes=[xo])
            p.tt(xo[:, 512:1024], bk1, s3bc[:, 512:1024], ALU.mult,
                 reads=rd + [s3bc], writes=[xo])
            p.tt(xo[:], xo[:], tmp4[ts4][:], ALU.add,
                 reads=[xo, tmp4[ts4]], writes=[xo])
            p.dma("sync", io["x1"].ap()[t0 + ts4 * 128:t0 + (ts4 + 1) * 128, :],
                  xo[:], reads=[xo], writes=[io["x1"]])
    p.barrier_reads("gpsimd", [io["x1"]])


def emit_ln_plain2(p, x_dram, T, xn_bf_dram, xn_f32_dram, bufs):
    nt = T // 128
    sx, ssq, sxn, st_ = bufs
    for tb in range(nt):
        i = tb % 2
        x_t, sq_t, xn_t, s = sx[i], ssq[i], sxn[i], st_[i]
        p.dma("sync", x_t[:], x_dram.ap()[tb * 128:(tb + 1) * 128, :],
              reads=[x_dram], writes=[x_t])
        mu, m2, mu2, var, rv, rstd = (s[:, 0:1], s[:, 1:2], s[:, 2:3],
                                      s[:, 3:4], s[:, 4:5], s[:, 5:6])
        p.reduce(mu, x_t[:], ALU.add, reads=[x_t], writes=[s])
        p.act(sq_t[:], x_t[:], AF.Square, reads=[x_t], writes=[sq_t])
        p.reduce(m2, sq_t[:], ALU.add, reads=[sq_t], writes=[s])
        p.ts(mu, mu, 1.0 / D, reads=[s], writes=[s])
        p.ts(m2, m2, 1.0 / D, reads=[s], writes=[s])
        p.tt(mu2, mu, mu, ALU.mult, reads=[s], writes=[s])
        p.tt(var, m2, mu2, ALU.subtract, reads=[s], writes=[s])
        p.ts(var, var, 1e-5, None, ALU.add, reads=[s], writes=[s])
        p.recip(rv, var, reads=[s], writes=[s])
        p.act(rstd, rv, AF.Sqrt, reads=[s], writes=[s])
        p.ts(sq_t[:], x_t[:], mu, rstd, ALU.subtract, ALU.mult,
             reads=[x_t, s], writes=[sq_t])
        p.copy(xn_t[:], sq_t[:], reads=[sq_t], writes=[xn_t])
        p.dma("sync", xn_f32_dram.ap()[tb * 128:(tb + 1) * 128, :], sq_t[:],
              reads=[sq_t], writes=[xn_f32_dram])
        p.dma("sync", xn_bf_dram.ap()[tb * 128:(tb + 1) * 128, :], xn_t[:],
              reads=[xn_t], writes=[xn_bf_dram])


def emit_l1_attn(p, io, T=4096, pfx="tm"):
    consts = io["consts"]
    nt = T // 128
    ntc = T // 512
    DECAY = 0.95

    ps2 = io["ps2"]
    pb = io["pb"]

    lnb = io.get("lnb") or (
        [p.sbuf(f"{pfx}_lnx{i}", [128, D], F32) for i in range(2)],
        [p.sbuf(f"{pfx}_lnsq{i}", [128, D], F32) for i in range(2)],
        [p.sbuf(f"{pfx}_lno{i}", [128, D], BF16) for i in range(2)],
        [p.sbuf(f"{pfx}_lnst{i}", [128, 8], F32) for i in range(2)])
    gb = p.sbuf(f"{pfx}_gb", [128, D], F32)
    bb = p.sbuf(f"{pfx}_bb", [128, D], F32)
    grow = p.sbuf(f"{pfx}_grow", [1, D], F32)
    brow = p.sbuf(f"{pfx}_brow", [1, D], F32)
    c095 = p.sbuf(f"{pfx}_c095", [128, 512], F32)
    cst = io.get("c095_src")
    p.dma("sync", c095[:], cst.ap(), reads=[cst], writes=[c095])

    wbfs = {}
    for nm in ("Wk", "Wv", "Wr", "Wo"):
        for l in range(2):
            d = p.dram(f"{pfx}_{nm}{l}_bf", [D, D], BF16)
            p.dma("gpsimd", d.ap(), io[nm].ap()[l], reads=[io[nm]], writes=[d],
                  tag="cast", chain="casts")
            wbfs[(nm, l)] = d

    slabs = {nm: p.sbuf_group(f"{pfx}_{nm}T", 8, [128, D], BF16)
             for nm in ("Wk", "Wv", "Wr", "Wo")}

    kA = p.sbuf(f"{pfx}_kA", [128, 512], F32)
    sA = p.sbuf(f"{pfx}_sA", [128, 512], F32)
    rA = p.sbuf(f"{pfx}_rA", [128, 512], F32)
    ych = p.sbuf_group(f"{pfx}_ych", 8, [128, 512], BF16)
    carry = p.sbuf(f"{pfx}_carry", [128, 8], F32)
    xtile = [p.sbuf(f"{pfx}_xt{i}", [128, D], F32) for i in range(2)]
    otile = [p.sbuf(f"{pfx}_ot{i}", [128, D], F32) for i in range(2)]
    nrmT = p.sbuf_group(f"{pfx}_nT", 8, [128, T], BF16)

    src = io["x"]
    for l in range(2):
        nrm_bf = p.dram(f"{pfx}_nrm{l}", [T, D], BF16)
        dst = io["aout"] if l == 1 else p.dram(f"{pfx}_sub{l}", [T, D], F32)
        p.dma("sync", grow[:], io["ln_w"].ap()[l].rearrange("(o d) -> o d", o=1),
              reads=[io["ln_w"]], writes=[grow])
        p.dma("sync", brow[:], io["ln_b"].ap()[l].rearrange("(o d) -> o d", o=1),
              reads=[io["ln_b"]], writes=[brow])
        bcast_row(p, consts, grow, grow.h, gb, D, pb[0])
        bcast_row(p, consts, brow, brow.h, bb, D, pb[1])
        sx, ssq, sxn, st_ = lnb
        for tb in range(nt):
            i = tb % 2
            x_t, sq_t, xn_t, s = sx[i], ssq[i], sxn[i], st_[i]
            p.dma("sync", x_t[:], src.ap()[tb * 128:(tb + 1) * 128, :],
                  reads=[src], writes=[x_t])
            mu, m2, mu2, var, rv, rstd = (s[:, 0:1], s[:, 1:2], s[:, 2:3],
                                          s[:, 3:4], s[:, 4:5], s[:, 5:6])
            p.reduce(mu, x_t[:], ALU.add, reads=[x_t], writes=[s])
            p.act(sq_t[:], x_t[:], AF.Square, reads=[x_t], writes=[sq_t])
            p.reduce(m2, sq_t[:], ALU.add, reads=[sq_t], writes=[s])
            p.ts(mu, mu, 1.0 / D, reads=[s], writes=[s])
            p.ts(m2, m2, 1.0 / D, reads=[s], writes=[s])
            p.tt(mu2, mu, mu, ALU.mult, reads=[s], writes=[s])
            p.tt(var, m2, mu2, ALU.subtract, reads=[s], writes=[s])
            p.ts(var, var, 1e-5, None, ALU.add, reads=[s], writes=[s])
            p.recip(rv, var, reads=[s], writes=[s])
            p.act(rstd, rv, AF.Sqrt, reads=[s], writes=[s])
            p.ts(sq_t[:], x_t[:], mu, rstd, ALU.subtract, ALU.mult,
                 reads=[x_t, s], writes=[sq_t])
            p.tt(sq_t[:], sq_t[:], gb[:], ALU.mult, reads=[sq_t, gb], writes=[sq_t])
            p.tt(xn_t[:], sq_t[:], bb[:], ALU.add, reads=[sq_t, bb], writes=[xn_t])
            p.dma("sync", nrm_bf.ap()[tb * 128:(tb + 1) * 128, :], xn_t[:],
                  reads=[xn_t], writes=[nrm_bf])
        for k in range(8):
            p.dmaT("sync", nrmT[k][:], nrm_bf.ap()[0:T, k * 128:(k + 1) * 128],
                   reads=[nrm_bf], writes=[nrmT[k]])
            for nm in ("Wk", "Wv", "Wr", "Wo"):
                p.dmaT("sync", slabs[nm][k][:],
                       wbfs[(nm, l)].ap()[0:D, k * 128:(k + 1) * 128],
                       reads=[wbfs[(nm, l)]], writes=[slabs[nm][k]])
        for tc in range(ntc):
            t0 = tc * 512
            for eb in range(8):
                for (nm, banki) in (("Wk", 0), ("Wv", 1), ("Wr", 2)):
                    bank = pb[banki]
                    for k in range(8):
                        p.mm(bank[0:128, 0:512],
                             slabs[nm][k][:, eb * 128:(eb + 1) * 128],
                             nrmT[k][:, t0:t0 + 512],
                             reads=[slabs[nm][k], nrmT[k]], writes=[bank],
                             start=(k == 0), stop=(k == 7))
                p.act(kA[:], pb[0][0:128, 0:512], AF.Copy, scale=1.0 - DECAY,
                      reads=[pb[0]], writes=[kA])
                p.tt(kA[:], kA[:], pb[1][0:128, 0:512], ALU.mult,
                     reads=[kA, pb[1]], writes=[kA])
                init = 0.0 if tc == 0 else carry[:, eb:eb + 1]
                rd = [kA, c095] if tc == 0 else [kA, c095, carry]
                p.scan(sA[:], c095[:], kA[:], init, ALU.mult, ALU.add,
                       reads=rd, writes=[sA])
                p.copy(carry[:, eb:eb + 1], sA[:, 511:512], reads=[sA], writes=[carry])
                p.act(rA[:], pb[2][0:128, 0:512], AF.Sigmoid,
                      reads=[pb[2]], writes=[rA])
                p.tt(ych[eb][:], rA[:], sA[:], ALU.mult,
                     reads=[rA, sA], writes=[ych[eb]])
            for ts in range(4):
                po = ps2[ts % 2]
                for eb in range(8):
                    for half in range(2):
                        p.mm(po[0:128, half * 512:(half + 1) * 512],
                             ych[eb][:, ts * 128:(ts + 1) * 128],
                             slabs["Wo"][eb][:, half * 512:(half + 1) * 512],
                             reads=[ych[eb], slabs["Wo"][eb]], writes=[po],
                             start=(eb == 0), stop=(eb == 7))
                xi = xtile[ts % 2]
                oo = otile[ts % 2]
                r0 = t0 + ts * 128
                p.dma("sync", xi[:], src.ap()[r0:r0 + 128, :],
                      reads=[src], writes=[xi])
                p.tt(oo[:], po[0:128, 0:1024], xi[:], ALU.add,
                     reads=[po, xi], writes=[oo])
                p.dma("sync", dst.ap()[r0:r0 + 128, :], oo[:],
                      reads=[oo], writes=[dst])
        src = dst
    return src


def emit_l1_ffn(p, io, T=4096):
    consts = io["consts"]
    nt = T // 128
    JP = T // 8
    NJB = JP // 128
    assert NJB >= 1
    FD = 4096
    ps2, pb = io["ps2"], io["pb"]

    iotacols = p.cin("c_iotacols", (np.arange(nt)[None, :] * 128 +
                                    np.arange(128)[:, None]).astype(np.int32))
    jcol = p.cin("c_jcol", (np.arange(NJB)[None, :] * 128 +
                            np.arange(128)[:, None] + 1).astype(np.float32))
    iota_jp = p.cin("c_iota_jp", np.broadcast_to(
        np.arange(1, JP + 1, dtype=np.float32), (128, JP)).copy())
    hdmod = np.arange(64) % 32
    invf = (1.0 / 10000.0 ** (2.0 * hdmod / 64.0)).astype(np.float32)
    invf1024 = p.cin("c_invf1024", np.broadcast_to(
        np.tile(invf, 16), (128, 1024)).copy())
    tri128 = p.cin("c_tri128", np.triu(np.ones((128, 128))).astype(BF))
    ones128b = p.cin("c_ones128b", np.ones((128, 128), dtype=BF))
    onescolb = p.cin("c_onescolb", np.ones((128, 1), dtype=BF))
    c1row_c = p.cin("c_c1row", np.ones((1, T), np.float32))
    padm_np = np.zeros((128, JP), np.float32)
    padm_np[127, :] = -1e9
    padm = p.cin("c_padm", padm_np)
    lc = {}
    for nm, t in (("iotacols", iotacols), ("jcol", jcol), ("iota_jp", iota_jp),
                  ("invf1024", invf1024), ("tri128", tri128),
                  ("ones128b", ones128b), ("onescolb", onescolb),
                  ("padm", padm)):
        s = p.sbuf("lc_" + nm, list(t.h.shape), t.h.dtype)
        p.dma("sync", s[:], t.ap(), reads=[t], writes=[s])
        lc[nm] = s

    wbf = {}
    for nm, shp in (("wq", [D, D]), ("wk", [D, D]),
                    ("wv", [D, D]), ("wo", [D, D]), ("w1", [D, FD]),
                    ("w2", [FD, D])):
        d = p.dram(f"f_{nm}_bf", shp, BF16)
        p.dma("gpsimd", d.ap(), io[nm].ap(), reads=[io[nm]], writes=[d],
              tag="cast", chain="casts")
        wbf[nm] = d

    wcol = p.sbuf("f_wcol", [128, nt], F32)
    mcol = p.sbuf("f_mcol", [128, nt], F32)
    mcolb = p.sbuf("f_mcolb", [128, nt], BF16)
    rpcol = p.sbuf("f_rpcol", [128, nt], F32)
    invsel = p.sbuf("f_invsel", [128, nt], F32)
    bis = p.sbuf("f_bis", [1, 8], F32)
    thrb = p.sbuf("f_thrb", [128, 1], F32)
    rbb = p.sbuf("f_rbb", [128, 1], F32)
    rb1 = p.sbuf("f_rb1", [1, 1], F32)
    selidx = p.sbuf("f_selidx", [128, NJB], I32)
    psel_f = p.sbuf("f_pself", [128, NJB], F32)
    psel_i = p.sbuf("f_pseli", [128, NJB], I32)
    wsel = p.sbuf("f_wsel", [128, NJB], F32)
    ln1b = p.sbuf("f_ln1b", [128, D], F32)
    ln2b = p.sbuf("f_ln2b", [128, D], F32)
    lrow = p.sbuf("f_lrow", [1, D], F32)

    wrow_dram = p.dram("f_wrow_d", [T], F32)
    selidx_dram = p.dram("f_selidx_d", [JP, 1], I32)
    hn_dram = p.dram("f_hn_d", [JP, D], BF16)
    hn2_dram = p.dram("f_hn2_d", [JP, D], BF16)
    g_dram = p.dram("f_g_d", [JP, FD], BF16)

    for (vec, dst) in ((io["ln1"], ln1b), (io["ln2"], ln2b)):
        p.dma("sync", lrow[:], vec.ap().rearrange("(o d) -> o d", o=1),
              reads=[vec], writes=[lrow])
        bcast_row(p, consts, lrow, lrow.h, dst, D, pb[0])
    p.dma("sync", rb1[:], io["router_b"].ap().rearrange("(o d) -> o d", o=1),
          reads=[io["router_b"]], writes=[rb1])
    p.mm(pb[1][0:128, 0:1], consts["ones1"][:], rb1[:],
         reads=[consts["ones1"], rb1], writes=[pb[1]])
    p.copy(rbb[:], pb[1][0:128, 0:1], reads=[pb[1]], writes=[rbb], engine="scalar")

    AR = 150 * 1024
    base = p.arena(AR)
    cur = [base]

    def aat(name, shape, dtype):
        st = p.at(cur[0], name, shape, dtype)
        itemsz = {F32: 4, BF16: 2, I32: 4}[dtype]
        cur[0] += shape[1] * itemsz
        assert cur[0] <= base + AR, f"arena overflow at {name} ({cur[0]-base} B)"
        return st

    PT = p.at_group(cur[0], "f_PT", NJB, [128, T], BF16)
    cur[0] += NJB * T * 2
    scr0 = cur[0]

    xt2 = [aat(f"f_xt{i}", [128, D], F32) for i in range(2)]
    xtm = aat("f_xtm", [128, D], F32)
    rwrow = p.sbuf("f_rwrow", [1, D], F32)
    rwbc = p.sbuf("f_rwbc", [128, D], F32)
    wrow = aat("f_wrow", [1, T], F32)
    rowtmp = aat("f_rowtmp", [1, T], F32)
    rrow = aat("f_rrow", [1, T], F32)
    c1row = aat("f_c1row", [1, T], F32)
    rpb = aat("f_rpb", [128, T], F32)
    phaseA = [*xt2, xtm]
    p.dma("sync", c1row[:], c1row_c.ap(), reads=[c1row_c], writes=[c1row])
    p.dma("sync", rwrow[:], io["router_w"].ap().transpose([1, 0]),
          reads=[io["router_w"]], writes=[rwrow])
    bcast_row(p, consts, rwrow, rwrow.h, rwbc, D, pb[2])
    for tb in range(nt):
        xt = xt2[tb % 2]
        p.dma("sync", xt[:], io["xn_f32"].ap()[tb * 128:(tb + 1) * 128, :],
              reads=[io["xn_f32"]], writes=[xt])
        p.tt(xtm[:], xt[:], rwbc[:], ALU.mult, reads=[xt, rwbc], writes=[xtm])
        p.reduce(wcol.h[:, tb:tb + 1], xtm[:], ALU.add, reads=[xtm], writes=[wcol])
        p.ts(wcol.h[:, tb:tb + 1], wcol.h[:, tb:tb + 1], rbb.h[:, 0:1], None,
             ALU.add, reads=[wcol, rbb], writes=[wcol])
        p.dma("sync",
              wrow_dram.ap()[tb * 128:(tb + 1) * 128].rearrange("(q o) -> q o", o=1),
              wcol.h[:, tb:tb + 1], reads=[wcol], writes=[wrow_dram])
    p.dma("sync", wrow[:], wrow_dram.ap().rearrange("(o d) -> o d", o=1),
          reads=[wrow_dram], writes=[wrow])

    lo, hi, mid, cnt, ge, ng = (bis[:, 0:1], bis[:, 1:2], bis[:, 2:3],
                                bis[:, 3:4], bis[:, 4:5], bis[:, 5:6])
    p.memset(bis[:], 0.0, writes=[bis])
    p.memset(bis[:, 0:1], -1e4, writes=[bis])
    p.memset(bis[:, 1:2], 1e4, writes=[bis])
    for it in range(40):
        p.tt(mid, lo, hi, ALU.add, reads=[bis], writes=[bis])
        p.ts(mid, mid, 0.5, None, ALU.mult, reads=[bis], writes=[bis])
        p.ts(rowtmp[:], wrow[:], mid, None, ALU.is_gt, reads=[wrow, bis],
             writes=[rowtmp])
        p.reduce(cnt, rowtmp[:], ALU.add, reads=[rowtmp], writes=[bis])
        p.ts(ge, cnt, float(JP) - 0.5, None, ALU.is_gt, reads=[bis], writes=[bis])
        t6, t7 = bis[:, 6:7], bis[:, 7:8]
        p.tt(t6, mid, lo, ALU.subtract, reads=[bis], writes=[bis])
        p.tt(t6, t6, ge, ALU.mult, reads=[bis], writes=[bis])
        p.tt(lo, lo, t6, ALU.add, reads=[bis], writes=[bis])
        p.ts(ng, ge, -1.0, 1.0, ALU.mult, ALU.add, reads=[bis], writes=[bis])
        p.tt(t7, mid, hi, ALU.subtract, reads=[bis], writes=[bis])
        p.tt(t7, t7, ng, ALU.mult, reads=[bis], writes=[bis])
        p.tt(hi, hi, t7, ALU.add, reads=[bis], writes=[bis])
    p.mm(pb[0][0:128, 0:1], consts["ones1"][:], hi, reads=[consts["ones1"], bis],
         writes=[pb[0]])
    p.copy(thrb[:], pb[0][0:128, 0:1], reads=[pb[0]], writes=[thrb], engine="scalar")

    p.ts(rowtmp[:], wrow[:], hi, None, ALU.is_gt, reads=[wrow, bis], writes=[rowtmp])
    p.scan(rrow[:], c1row[:], rowtmp[:], 0.0, ALU.mult, ALU.add,
           reads=[c1row, rowtmp], writes=[rrow])
    p.tt(rrow[:], rrow[:], rowtmp[:], ALU.mult, reads=[rrow, rowtmp], writes=[rrow])
    p.tt(rrow[:], rrow[:], rowtmp[:], ALU.add, reads=[rrow, rowtmp], writes=[rrow])
    p.ts(rrow[:], rrow[:], -1.0, None, ALU.add, reads=[rrow], writes=[rrow])
    for c in range(T // 512):
        p.mm(pb[1][0:128, 0:512], consts["ones1"][:], rrow[0:1, c * 512:(c + 1) * 512],
             reads=[consts["ones1"], rrow], writes=[pb[1]])
        p.copy(rpb[:, c * 512:(c + 1) * 512], pb[1][0:128, 0:512],
               reads=[pb[1]], writes=[rpb], engine="scalar")
    for jb in range(NJB):
        p.ts(PT[jb][:], rpb[:], lc["jcol"][:, jb:jb + 1], None, ALU.is_equal,
             reads=[rpb, lc["jcol"]], writes=[PT[jb]])
    p.ts(mcol[:], wcol[:], thrb[:, 0:1], None, ALU.is_gt, reads=[wcol, thrb],
         writes=[mcol])
    p.copy(mcolb[:], mcol[:], reads=[mcol], writes=[mcolb])
    for tb in range(nt):
        for c in range(tb + 1):
            lhs = lc["tri128"] if c == tb else lc["ones128b"]
            p.mm(pb[0][0:128, 0:1], lhs[:], mcolb[:, c:c + 1],
                 reads=[lhs, mcolb], writes=[pb[0]],
                 start=(c == 0), stop=(c == tb))
        p.copy(rpcol[:, tb:tb + 1], pb[0][0:128, 0:1], reads=[pb[0]],
               writes=[rpcol], engine="scalar")
    p.tt(rpcol[:], rpcol[:], mcol[:], ALU.mult, reads=[rpcol, mcol], writes=[rpcol])
    p.tt(rpcol[:], rpcol[:], mcol[:], ALU.add, reads=[rpcol, mcol], writes=[rpcol])
    p.ts(rpcol[:], rpcol[:], -1.0, None, ALU.add, reads=[rpcol], writes=[rpcol])
    p.ts(invsel[:], mcol[:], -1.0, 1.0, ALU.mult, ALU.add, reads=[mcol],
         writes=[invsel])

    zt = p.sbuf("f_zt", [128, NJB], I32)
    p.memset(zt[:], 0, writes=[zt])
    p.dma("sync", selidx_dram.ap().rearrange("(q b) o -> q (b o)", b=NJB),
          zt[:], reads=[zt], writes=[selidx_dram])
    dst_f = p.sbuf("f_dstf", [128, 4], F32)
    dst_i = p.sbuf("f_dsti", [128, 1], I32)
    bigc = p.sbuf("f_bigc", [128, 1], F32)
    p.memset(bigc[:], 1e9, writes=[bigc])
    for tb in range(nt):
        j_, jd, jm, fl = dst_f[:, 0:1], dst_f[:, 1:2], dst_f[:, 2:3], dst_f[:, 3:4]
        p.ts(j_, rpcol[:, tb:tb + 1], -1.0, None, ALU.add, reads=[rpcol],
             writes=[dst_f])
        p.ts(j_, j_, 0.0, None, ALU.max, reads=[dst_f], writes=[dst_f])
        p.ts(jd, j_, 1.0 / 128, None, ALU.mult, reads=[dst_f], writes=[dst_f])
        p.copy(dst_i[:], jd, reads=[dst_f], writes=[dst_i])
        p.copy(jd, dst_i[:], reads=[dst_i], writes=[dst_f])
        p.stt(jm, jd, -128.0, j_, ALU.mult, ALU.add, reads=[dst_f], writes=[dst_f])
        p.ts(fl, jm, 0.0, None, ALU.is_lt, reads=[dst_f], writes=[dst_f])
        p.tt(jd, jd, fl, ALU.subtract, reads=[dst_f], writes=[dst_f])
        p.ts(fl, fl, 128.0, None, ALU.mult, reads=[dst_f], writes=[dst_f])
        p.tt(jm, jm, fl, ALU.add, reads=[dst_f], writes=[dst_f])
        p.ts(jm, jm, float(NJB), None, ALU.mult, reads=[dst_f], writes=[dst_f])
        p.tt(jm, jm, jd, ALU.add, reads=[dst_f], writes=[dst_f])
        p.tt(jm, jm, mcol[:, tb:tb + 1], ALU.mult, reads=[dst_f, mcol],
             writes=[dst_f])
        p.ts(fl, mcol[:, tb:tb + 1], -1e9, 1e9, ALU.mult, ALU.add,
             reads=[mcol], writes=[dst_f])
        p.tt(jm, jm, fl, ALU.add, reads=[dst_f], writes=[dst_f])
        p.copy(dst_i[:], jm, reads=[dst_f], writes=[dst_i])
        p.indirect(selidx_dram.ap(),
                   IndirectOffsetOnAxis(dst_i.h[:], 0),
                   lc["iotacols"][:, tb:tb + 1], None,
                   reads=[dst_i, lc["iotacols"], selidx_dram],
                   writes=[selidx_dram], bounds_check=JP - 1)
    p.dma("sync", selidx[:], selidx_dram.ap().rearrange("(q b) o -> q (b o)", b=NJB),
          reads=[selidx_dram], writes=[selidx])
    for jb in range(NJB):
        p.indirect(psel_i.h[:, jb:jb + 1], None,
                   io["posb"].ap().rearrange("(q o) -> q o", o=1),
                   IndirectOffsetOnAxis(selidx.h[:, jb:jb + 1], 0),
                   reads=[selidx, io["posb"]], writes=[psel_i], tag="gpos")
        p.indirect(wsel.h[:, jb:jb + 1], None,
                   wrow_dram.ap().rearrange("(q o) -> q o", o=1),
                   IndirectOffsetOnAxis(selidx.h[:, jb:jb + 1], 0),
                   reads=[selidx, wrow_dram], writes=[wsel], tag="gw")
    p.copy(psel_f[:], psel_i[:], reads=[psel_i], writes=[psel_f])
    phaseA += [wrow, rowtmp, rrow, c1row, rpb]
    return emit_l1_ffn_part2(p, io, T, locals())


def emit_l1_ffn_part2(p, io, T, env):
    consts = env["consts"]
    lc, wbf, pb, ps2 = env["lc"], env["wbf"], env["pb"], env["ps2"]
    NJB, JP, nt = env["NJB"], env["JP"], env["nt"]
    FD = env["FD"]
    PT, base, scr0, aat, cur = env["PT"], env["base"], env["scr0"], env["aat"], env["cur"]
    phaseA = env["phaseA"]
    selidx, psel_f, wsel = env["selidx"], env["psel_f"], env["wsel"]
    mcol, rpcol, invsel = env["mcol"], env["rpcol"], env["invsel"]
    ln1b, ln2b = env["ln1b"], env["ln2b"]
    hn_dram, hn2_dram, g_dram = env["hn_dram"], env["hn2_dram"], env["g_dram"]
    AR = env["AR"]

    mselT = [p.sbuf(f"f_mselT{kb}", [128, JP], F32) for kb in range(NJB)]
    fst = p.sbuf("f_fst", [128, 8], F32)
    zq = p.sbuf("f_zq", [128, 2 * NJB], F32)
    fsti = p.sbuf("f_fsti", [128, D], I32)

    q_dram = p.dram("f_q_d", [JP, D], BF16)
    k_dram = p.dram("f_k_d", [JP, D], BF16)
    oc_dram = p.dram("f_oc_d", [JP, D], BF16)

    cur[0] = scr0
    mrows = aat("f_mrows", [128, T], F32)
    mrowsT = p.at_group(cur[0], "f_mrT", nt, [128, JP], BF16)
    cur[0] += nt * JP * 2
    ptile = [aat(f"f_ptile{i}", [128, JP], BF16) for i in range(2)]
    p.claim([mrows] + mrowsT + ptile, phaseA)
    for jb in range(NJB):
        p.indirect(mrows[:], None, io["maskb"].ap(),
                   IndirectOffsetOnAxis(selidx.h[:, jb:jb + 1], 0),
                   reads=[selidx, io["maskb"]], writes=[mrows], tag="gmask")
        for sb in range(nt):
            bank = pb[sb % 2]
            p.transpose(bank.h[0:128, 0:128], mrows[:, sb * 128:(sb + 1) * 128],
                        consts["ident_f32"][:], reads=[mrows, consts["ident_f32"]],
                        writes=[bank])
            p.copy(mrowsT[sb][:, jb * 128:(jb + 1) * 128], bank[0:128, 0:128],
                   reads=[bank], writes=[mrowsT[sb]], engine="scalar")
    for sb in range(nt):
        pt = ptile[sb % 2]
        p.ts(pt[:], lc["iota_jp"][:], rpcol[:, sb:sb + 1], None, ALU.is_equal,
             reads=[lc["iota_jp"], rpcol], writes=[pt])
        for kb in range(NJB):
            p.mm(pb[kb][0:128, 0:JP], pt[:, kb * 128:(kb + 1) * 128],
                 mrowsT[sb][:], reads=[pt, mrowsT[sb]], writes=[pb[kb]],
                 start=(sb == 0), stop=(sb == nt - 1))
    for kb in range(NJB):
        p.copy(mselT[kb][:], pb[kb][0:128, 0:JP], reads=[pb[kb]],
               writes=[mselT[kb]])
    p.tt(mselT[NJB - 1][:], mselT[NJB - 1][:], lc["padm"][:], ALU.add,
         reads=[mselT[NJB - 1], lc["padm"]], writes=[mselT[NJB - 1]])
    for kb in range(NJB):
        p.memset(mselT[kb][:, JP - 1:JP], 0.0, writes=[mselT[kb]])

    cur[0] = scr0
    xsel32 = [aat(f"f_xsel{jb}", [128, D], F32) for jb in range(NJB)]
    x1tok = xsel32
    tmpd = aat("f_tmpd", [128, D], F32)
    hnT = p.at_group(cur[0], "f_hnT", 8, [128, JP], BF16)
    cur[0] += 8 * JP * 2
    th_ = aat("f_th", [128, D], F32)
    snt = aat("f_snt", [128, D], F32)
    qbf = [aat(f"f_qbf{i}", [128, D], BF16) for i in range(2)]
    vtok = [aat(f"f_vtok{jb}", [128, D], BF16) for jb in range(NJB)]
    etmp = [aat(f"f_etmp{i}", [128, max(JP, 512)], F32) for i in range(2)]
    qT = p.at_group(cur[0], "f_qT", 8, [128, JP], BF16)
    cur[0] += 8 * JP * 2
    kT = p.at_group(cur[0], "f_kT", 8, [128, JP], BF16)
    cur[0] += 8 * JP * 2
    dreg0 = cur[0]
    wslab = p.at_group(cur[0], "f_wslab", 8, [128, D], BF16)
    cur[0] += 8 * D * 2
    qsc = aat("f_qsc", [128, D], F32)
    cst_ = aat("f_cst", [128, D], F32)
    rot = aat("f_rot", [128, D], F32)
    etile = [aat(f"f_et{kb}", [128, JP], BF16) for kb in range(NJB)]
    ocat = [aat(f"f_ocat{jb}", [128, D], F32) for jb in range(NJB)]
    phaseC = (xsel32 + [tmpd, th_, snt, qsc, cst_, rot] + hnT + wslab + qbf +
              vtok + etile + etmp + qT + kT + ocat)
    p.claim(phaseC, [mrows] + mrowsT + ptile)

    for jb in range(NJB):
        p.indirect(xsel32[jb][:], None, io["xn_f32"].ap(),
                   IndirectOffsetOnAxis(selidx.h[:, jb:jb + 1], 0),
                   reads=[selidx, io["xn_f32"]], writes=[xsel32[jb]], tag="gx")

    def rms_to_dram(src_list, lnb, dram):
        for jb in range(NJB):
            s = fst
            m2, rv, rstd = s[:, 0:1], s[:, 1:2], s[:, 2:3]
            p.act(tmpd[:], src_list[jb][:], AF.Square, reads=[src_list[jb]],
                  writes=[tmpd])
            p.reduce(m2, tmpd[:], ALU.add, reads=[tmpd], writes=[s])
            p.ts(m2, m2, 1.0 / D, 1e-6, ALU.mult, ALU.add, reads=[s], writes=[s])
            p.recip(rv, m2, reads=[s], writes=[s])
            p.act(rstd, rv, AF.Sqrt, reads=[s], writes=[s])
            p.ts(tmpd[:], src_list[jb][:], rstd, None, ALU.mult,
                 reads=[src_list[jb], s], writes=[tmpd])
            b = qbf[jb % 2]
            p.tt(b[:], tmpd[:], lnb[:], ALU.mult, reads=[tmpd, lnb], writes=[b])
            p.dma("sync", dram.ap()[jb * 128:(jb + 1) * 128, :], b[:],
                  reads=[b], writes=[dram])

    rms_to_dram(xsel32, ln1b, hn_dram)
    for k in range(8):
        p.dmaT("sync", hnT[k][:], hn_dram.ap()[0:JP, k * 128:(k + 1) * 128],
               reads=[hn_dram], writes=[hnT[k]])

    TWO_PI = float(2 * np.pi)
    INV2PI = float(1.0 / TWO_PI)
    C1 = 6.28125
    C2 = float(TWO_PI - C1)

    def sin_reduced(dst, phase, jb):
        u, uf, cor = th_, rot, tmpd
        p.ts(u[:], lc["invf1024"][:], psel_f[:, jb:jb + 1], None, ALU.mult,
             reads=[lc["invf1024"], psel_f], writes=[u])
        p.ts(u[:], u[:], INV2PI, phase * INV2PI, ALU.mult, ALU.add,
             reads=[u], writes=[u])
        p.copy(fsti[:], u[:], reads=[u], writes=[fsti])
        p.copy(uf[:], fsti[:], reads=[fsti], writes=[uf])
        p.tt(cor[:], u[:], uf[:], ALU.is_lt, reads=[u, uf], writes=[cor])
        p.tt(uf[:], uf[:], cor[:], ALU.subtract, reads=[uf, cor], writes=[uf])
        p.ts(u[:], lc["invf1024"][:], psel_f[:, jb:jb + 1], None, ALU.mult,
             reads=[lc["invf1024"], psel_f], writes=[u])
        p.ts(u[:], u[:], 1.0, phase, ALU.mult, ALU.add, reads=[u], writes=[u])
        p.stt(cor[:], uf[:], -C1, u[:], ALU.mult, ALU.add, reads=[uf, u],
              writes=[cor])
        p.stt(cor[:], uf[:], -C2, cor[:], ALU.mult, ALU.add, reads=[uf, cor],
              writes=[cor])
        p.ts(cor[:], cor[:], float(TWO_PI - 1e-6), 0.0, ALU.min, ALU.max,
             reads=[cor], writes=[cor])
        p.act(dst[:], cor[:], AF.Sin, bias=float(-np.pi), reads=[cor], writes=[dst])

    for (wname, dram) in (("wq", q_dram), ("wk", k_dram)):
        for k in range(8):
            p.dma("sync", wslab[k][:], wbf[wname].ap()[k * 128:(k + 1) * 128, :],
                  reads=[wbf[wname]], writes=[wslab[k]])
        for jb in range(NJB):
            po = ps2[jb % 2]
            for half in range(2):
                for k in range(8):
                    p.mm(po[0:128, half * 512:(half + 1) * 512],
                         hnT[k][:, jb * 128:(jb + 1) * 128],
                         wslab[k][:, half * 512:(half + 1) * 512],
                         reads=[hnT[k], wslab[k]], writes=[po],
                         start=(k == 0), stop=(k == 7))
            p.copy(qsc[:], po[0:128, 0:D], reads=[po], writes=[qsc])
            sin_reduced(snt, 0.0, jb)
            sin_reduced(cst_, float(np.pi / 2), jb)
            qv = qsc.h[:].rearrange("q (h w) -> q h w", w=64)
            rv_ = rot.h[:].rearrange("q (h w) -> q h w", w=64)
            p.ts(rv_[:, :, 0:32], qv[:, :, 32:64], -1.0, None, ALU.mult,
                 reads=[qsc], writes=[rot])
            p.copy(rv_[:, :, 32:64], qv[:, :, 0:32], reads=[qsc], writes=[rot])
            b = qbf[jb % 2]
            p.tt(qsc[:], qsc[:], cst_[:], ALU.mult, reads=[qsc, cst_], writes=[qsc])
            p.tt(rot[:], rot[:], snt[:], ALU.mult, reads=[rot, snt], writes=[rot])
            p.tt(b[:], qsc[:], rot[:], ALU.add, reads=[qsc, rot], writes=[b])
            p.dma("sync", dram.ap()[jb * 128:(jb + 1) * 128, :], b[:],
                  reads=[b], writes=[dram])
    for k in range(8):
        p.dma("sync", wslab[k][:], wbf["wv"].ap()[k * 128:(k + 1) * 128, :],
              reads=[wbf["wv"]], writes=[wslab[k]])
    for jb in range(NJB):
        po = ps2[jb % 2]
        for half in range(2):
            for k in range(8):
                p.mm(po[0:128, half * 512:(half + 1) * 512],
                     hnT[k][:, jb * 128:(jb + 1) * 128],
                     wslab[k][:, half * 512:(half + 1) * 512],
                     reads=[hnT[k], wslab[k]], writes=[po],
                     start=(k == 0), stop=(k == 7))
        p.copy(vtok[jb][:], po[0:128, 0:D], reads=[po], writes=[vtok[jb]])
    for k in range(8):
        p.dmaT("sync", qT[k][:], q_dram.ap()[0:JP, k * 128:(k + 1) * 128],
               reads=[q_dram], writes=[qT[k]])
        p.dmaT("sync", kT[k][:], k_dram.ap()[0:JP, k * 128:(k + 1) * 128],
               reads=[k_dram], writes=[kT[k]])
    for h in range(16):
        sl = h // 2
        r0 = (h % 2) * 64
        for kb in range(NJB):
            bank = pb[kb % 2]
            p.mm(bank[0:128, 0:JP], kT[sl][r0:r0 + 64, kb * 128:(kb + 1) * 128],
                 qT[sl][r0:r0 + 64, 0:JP], reads=[kT[sl], qT[sl]], writes=[bank],
                 start=True, stop=True)
            e = etmp[kb % 2]
            p.stt(e.h[:, 0:JP], bank[0:128, 0:JP], 0.125, mselT[kb][:],
                  ALU.mult, ALU.add, reads=[bank, mselT[kb]], writes=[e])
            p.act(etile[kb][:], e.h[:, 0:JP], AF.Exp, reads=[e], writes=[etile[kb]])
        for qb in range(NJB):
            for kb in range(NJB):
                p.mm(pb[2][0:128, qb:qb + 1], etile[kb][:, qb * 128:(qb + 1) * 128],
                     lc["onescolb"][:], reads=[etile[kb], lc["onescolb"]],
                     writes=[pb[2]],
                     start=(kb == 0 and qb == 0),
                     stop=(kb == NJB - 1 and qb == NJB - 1))
        p.copy(zq[:, 0:NJB], pb[2][0:128, 0:NJB], reads=[pb[2]], writes=[zq])
        p.recip(zq[:, NJB:2 * NJB], zq[:, 0:NJB], reads=[zq], writes=[zq])
        for qb in range(NJB):
            bank = pb[3]
            for kb in range(NJB):
                p.mm(bank[0:128, 0:64], etile[kb][:, qb * 128:(qb + 1) * 128],
                     vtok[kb][:, h * 64:(h + 1) * 64],
                     reads=[etile[kb], vtok[kb]], writes=[bank],
                     start=(kb == 0), stop=(kb == NJB - 1))
            p.act(ocat[qb][:, h * 64:(h + 1) * 64], bank[0:128, 0:64], AF.Copy,
                  scale=zq.h[:, NJB + qb:NJB + qb + 1],
                  reads=[bank, zq], writes=[ocat[qb]])
    for jb in range(NJB):
        b = qbf[jb % 2]
        p.copy(b[:], ocat[jb][:], reads=[ocat[jb]], writes=[b])
        p.dma("sync", oc_dram.ap()[jb * 128:(jb + 1) * 128, :], b[:],
              reads=[b], writes=[oc_dram])
    for k in range(8):
        p.dmaT("sync", qT[k][:], oc_dram.ap()[0:JP, k * 128:(k + 1) * 128],
               reads=[oc_dram], writes=[qT[k]])
        p.dma("sync", wslab[k][:], wbf["wo"].ap()[k * 128:(k + 1) * 128, :],
              reads=[wbf["wo"]], writes=[wslab[k]])
    for jb in range(NJB):
        po = ps2[jb % 2]
        for half in range(2):
            for k in range(8):
                p.mm(po[0:128, half * 512:(half + 1) * 512],
                     qT[k][:, jb * 128:(jb + 1) * 128],
                     wslab[k][:, half * 512:(half + 1) * 512],
                     reads=[qT[k], wslab[k]], writes=[po],
                     start=(k == 0), stop=(k == 7))
        p.tt(x1tok[jb][:], po[0:128, 0:D], xsel32[jb][:], ALU.add,
             reads=[po, xsel32[jb]], writes=[x1tok[jb]])
    rms_to_dram(x1tok, ln2b, hn2_dram)
    for k in range(8):
        p.dmaT("sync", hnT[k][:], hn2_dram.ap()[0:JP, k * 128:(k + 1) * 128],
               reads=[hn2_dram], writes=[hnT[k]])
    cur[0] = dreg0
    wpart = [aat(f"f_wpart{i}", [128, 512], BF16) for i in range(2)]
    du = [aat(f"f_du{i}", [128, 512], F32) for i in range(2)]
    cb = [aat(f"f_cb{i}", [128, D], F32) for i in range(6)]
    gTs = [aat(f"f_gTs{i}", [128, JP], BF16) for i in range(2)]
    dead = [qsc, cst_, rot] + etile + ocat + wslab
    p.claim(wpart + du + cb + gTs, dead)
    proc = vtok
    for fc in range(FD // 512):
        for k in range(8):
            w = wpart[k % 2]
            p.dma("sync", w[:], wbf["w1"].ap()[k * 128:(k + 1) * 128,
                                               fc * 512:(fc + 1) * 512],
                  reads=[wbf["w1"]], writes=[w])
            for jb in range(NJB):
                p.mm(pb[jb][0:128, 0:512], hnT[k][:, jb * 128:(jb + 1) * 128],
                     w[:], reads=[hnT[k], w], writes=[pb[jb]],
                     start=(k == 0), stop=(k == 7))
        for jb in range(NJB):
            u = du[jb % 2]
            p.copy(u[:], pb[jb][0:128, 0:512], reads=[pb[jb]], writes=[u])
            a = th_.h[:, 0:512]
            bsc = snt.h[:, 0:512]
            p.tt(a, u[:], u[:], ALU.mult, reads=[u], writes=[th_])
            p.tt(a, a, u[:], ALU.mult, reads=[th_, u], writes=[th_])
            p.stt(a, a, 0.044715, u[:], ALU.mult, ALU.add, reads=[th_, u],
                  writes=[th_])
            p.act(bsc, a, AF.Tanh, scale=float(np.sqrt(2.0 / np.pi)),
                  reads=[th_], writes=[snt])
            p.ts(bsc, bsc, 0.5, 0.5, ALU.mult, ALU.add, reads=[snt], writes=[snt])
            gb_ = qbf[jb % 2]
            p.tt(gb_[:, 0:512], bsc, u[:], ALU.mult, reads=[snt, u], writes=[gb_])
            p.dma("sync", g_dram.ap()[jb * 128:(jb + 1) * 128,
                                      fc * 512:(fc + 1) * 512], gb_[:, 0:512],
                  reads=[gb_], writes=[g_dram])
    for dh in range(2):
        for k32 in range(FD // 128):
            gt = gTs[k32 % 2]
            p.dmaT("sync", gt[:], g_dram.ap()[0:JP, k32 * 128:(k32 + 1) * 128],
                   reads=[g_dram], writes=[gt])
            w = wpart[k32 % 2]
            p.dma("sync", w[:], wbf["w2"].ap()[k32 * 128:(k32 + 1) * 128,
                                               dh * 512:(dh + 1) * 512],
                  reads=[wbf["w2"]], writes=[w])
            for jb in range(NJB):
                p.mm(pb[jb][0:128, 0:512], gt[:, jb * 128:(jb + 1) * 128], w[:],
                     reads=[gt, w], writes=[pb[jb]],
                     start=(k32 == 0), stop=(k32 == FD // 128 - 1))
        for jb in range(NJB):
            u = du[jb % 2]
            p.tt(u[:], pb[jb][0:128, 0:512], x1tok[jb][:, dh * 512:(dh + 1) * 512],
                 ALU.add, reads=[pb[jb], x1tok[jb]], writes=[u])
            p.ts(proc[jb][:, dh * 512:(dh + 1) * 512], u[:], wsel.h[:, jb:jb + 1],
                 None, ALU.mult, reads=[u, wsel], writes=[proc[jb]])
    for ts in range(nt):
        po = ps2[ts % 2]
        for half in range(2):
            for jb in range(NJB):
                p.mm(po[0:128, half * 512:(half + 1) * 512],
                     PT[jb][:, ts * 128:(ts + 1) * 128],
                     proc[jb][:, half * 512:(half + 1) * 512],
                     reads=[PT[jb], proc[jb]], writes=[po],
                     start=(jb == 0), stop=(jb == NJB - 1))
        xf, x1_, ao = cb[(ts % 2) * 3], cb[(ts % 2) * 3 + 1], cb[(ts % 2) * 3 + 2]
        r0 = ts * 128
        p.dma("sync", xf[:], io["xn_f32"].ap()[r0:r0 + 128, :],
              reads=[io["xn_f32"]], writes=[xf])
        p.dma("sync", x1_[:], io["x1"].ap()[r0:r0 + 128, :],
              reads=[io["x1"]], writes=[x1_])
        p.dma("sync", ao[:], io["aout"].ap()[r0:r0 + 128, :],
              reads=[io["aout"]], writes=[ao])
        p.ts(xf[:], xf[:], invsel.h[:, ts:ts + 1], None, ALU.mult,
             reads=[xf, invsel], writes=[xf])
        p.tt(xf[:], xf[:], x1_[:], ALU.add, reads=[xf, x1_], writes=[xf])
        p.tt(xf[:], xf[:], ao[:], ALU.add, reads=[xf, ao], writes=[xf])
        p.tt(x1_[:], xf[:], po[0:128, 0:D], ALU.add, reads=[xf, po], writes=[x1_])
        p.dma("sync", io["out"].ap()[r0:r0 + 128, :], x1_[:],
              reads=[x1_], writes=[io["out"]])
    return dict(selidx=selidx, wsel=wsel, psel_f=psel_f, mcol=mcol,
                rpcol=rpcol, x1tok=x1tok, proc=proc, mselT=mselT, zq=zq,
                bis=env["bis"], wrow_dram=env["wrow_dram"], PT=PT, vtok=vtok,
                hnT=hnT, etile=etile)


T_FULL = 4096
D_MODEL = 1024


def _build_p1():
    p = Prog("mk_l0")
    io = {
        "x": p.din("x", [T_FULL, 1024], F32),
        "qp": p.din("qp", [16, 16, 1024], F32),
        "kp": p.din("kp", [16, 16, 1024], F32),
        "vp": p.din("vp", [16, 64, 1024], F32),
        "op": p.din("op", [1024, 16, 64], F32),
        "w1_w": p.din("w1_w", [4096, 1024], F32),
        "w1_s": p.din("w1_s", [4096], F32),
        "w2_w": p.din("w2_w", [4096, 1024], F32),
        "w2_s": p.din("w2_s", [4096], F32),
        "h1_w": p.din("h1_w", [4096, 4096], F32),
        "h1_s": p.din("h1_s", [4096], F32),
        "h2_w": p.din("h2_w", [4096, 4096], F32),
        "h2_s": p.din("h2_s", [4096], F32),
        "w3_w": p.din("w3_w", [1024, 4096], F32),
        "w3_s": p.din("w3_s", [1024], F32),
        "x1": p.dout("x1", [T_FULL, 1024], F32),
    }
    emit_l0(p, io, T=T_FULL)
    p.barrier_reads("gpsimd", [io["x1"]])
    return p.finalize(), p.extra_inputs


def _build_p2():
    p = Prog("mk_l1a")
    io = {
        "x1": p.din("x1", [T_FULL, 1024], F32),
        "ln_w": p.din("ln_w", [2, 1024], F32),
        "ln_b": p.din("ln_b", [2, 1024], F32),
        "Wk": p.din("Wk", [2, 1024, 1024], F32),
        "Wv": p.din("Wv", [2, 1024, 1024], F32),
        "Wr": p.din("Wr", [2, 1024, 1024], F32),
        "Wo": p.din("Wo", [2, 1024, 1024], F32),
        "aout": p.dout("aout", [T_FULL, 1024], F32),
        "xn_bf": p.dout("xn_bf", [T_FULL, 1024], BF16),
        "xn_f32": p.dout("xn_f32", [T_FULL, 1024], F32),
    }
    io["consts"] = common_consts(p)
    io["ps2"] = [p.psum(f"ps2_{k}", [128, 1024], F32) for k in range(2)]
    io["pb"] = [p.psum(f"psb_{k}", [128, 512], F32) for k in range(4)]
    io["c095_src"] = p.cin("c095_src", np.full((128, 512), 0.95, np.float32))
    lnb = ([p.sbuf(f"bln_x{i}", [128, 1024], F32) for i in range(2)],
           [p.sbuf(f"bln_sq{i}", [128, 1024], F32) for i in range(2)],
           [p.sbuf(f"bln_o{i}", [128, 1024], BF16) for i in range(2)],
           [p.sbuf(f"bln_st{i}", [128, 8], F32) for i in range(2)])
    emit_ln_plain2(p, io["x1"], T_FULL, io["xn_bf"], io["xn_f32"], lnb)
    io["x"] = io["xn_f32"]
    io["lnb"] = lnb
    emit_l1_attn(p, io, T=T_FULL)
    p.barrier_reads("gpsimd", [io["aout"], io["xn_bf"], io["xn_f32"]])
    return p.finalize(), p.extra_inputs


def _build_p3():
    p = Prog("mk_l1f")
    io = {
        "xn_bf": p.din("xn_bf", [T_FULL, 1024], BF16),
        "xn_f32": p.din("xn_f32", [T_FULL, 1024], F32),
        "x1": p.din("x1", [T_FULL, 1024], F32),
        "aout": p.din("aout", [T_FULL, 1024], F32),
        "maskb": p.din("maskb", [T_FULL, T_FULL], F32),
        "posb": p.din("posb", [T_FULL], I32),
        "router_w": p.din("router_w", [1024, 1], F32),
        "router_b": p.din("router_b", [1], F32),
        "wq": p.din("wq", [1024, 1024], F32),
        "wk": p.din("wk", [1024, 1024], F32),
        "wv": p.din("wv", [1024, 1024], F32),
        "wo": p.din("wo", [1024, 1024], F32),
        "w1": p.din("w1", [1024, 4096], F32),
        "w2": p.din("w2", [4096, 1024], F32),
        "ln1": p.din("ln1", [1024], F32),
        "ln2": p.din("ln2", [1024], F32),
        "out": p.dout("out", [T_FULL, 1024], F32),
    }
    io["consts"] = common_consts(p)
    io["ps2"] = [p.psum(f"ps2_{k}", [128, 1024], F32) for k in range(2)]
    io["pb"] = [p.psum(f"psb_{k}", [128, 512], F32) for k in range(4)]
    emit_l1_ffn(p, io, T=T_FULL)
    p.barrier_reads("gpsimd", [io["out"]])
    return p.finalize(), p.extra_inputs


def kernel(**inputs):
    from concourse.bass_utils import run_bass_kernel_spmd
    inp = {k: np.asarray(v) for k, v in inputs.items()}
    cores = list(range(8))

    nc1, extra1 = _build_p1()
    maps1 = []
    for b in cores:
        maps1.append(dict(
            x=np.ascontiguousarray(inp["L0_attn_x"][b]),
            qp=inp["L0_attn_query_proj"], kp=inp["L0_attn_key_proj"],
            vp=inp["L0_attn_value_proj"], op=inp["L0_attn_output_proj"],
            w1_w=inp["L0_ffn_w1_w"], w1_s=inp["L0_ffn_w1_s"],
            w2_w=inp["L0_ffn_w2_w"], w2_s=inp["L0_ffn_w2_s"],
            h1_w=inp["L0_ffn_h1_w"], h1_s=inp["L0_ffn_h1_s"],
            h2_w=inp["L0_ffn_h2_w"], h2_s=inp["L0_ffn_h2_s"],
            w3_w=inp["L0_ffn_w3_w"], w3_s=inp["L0_ffn_w3_s"],
            **extra1))
    r1 = run_bass_kernel_spmd(nc1, maps1, core_ids=cores)
    x1 = [np.asarray(r1.results[b]["x1"]) for b in cores]

    nc2, extra2 = _build_p2()
    maps2 = []
    for b in cores:
        maps2.append(dict(
            x1=x1[b], ln_w=inp["L1_attn_ln_w"], ln_b=inp["L1_attn_ln_b"],
            Wk=inp["L1_attn_Wk"], Wv=inp["L1_attn_Wv"], Wr=inp["L1_attn_Wr"],
            Wo=inp["L1_attn_Wo"], **extra2))
    r2 = run_bass_kernel_spmd(nc2, maps2, core_ids=cores)

    nc3, extra3 = _build_p3()
    maps3 = []
    for b in cores:
        bc = min(b, 3)
        maps3.append(dict(
            xn_bf=np.asarray(r2.results[b]["xn_bf"]),
            xn_f32=np.asarray(r2.results[b]["xn_f32"]),
            x1=x1[b], aout=np.asarray(r2.results[b]["aout"]),
            maskb=np.ascontiguousarray(inp["L1_ffn_attention_mask"][bc, 0]),
            posb=np.ascontiguousarray(inp["L1_ffn_position_ids"][bc]),
            router_w=inp["L1_ffn_router_w"], router_b=inp["L1_ffn_router_b"],
            wq=inp["L1_ffn_wq"], wk=inp["L1_ffn_wk"], wv=inp["L1_ffn_wv"],
            wo=inp["L1_ffn_wo"], w1=inp["L1_ffn_w1"], w2=inp["L1_ffn_w2"],
            ln1=inp["L1_ffn_ln1"], ln2=inp["L1_ffn_ln2"], **extra3))
    r3 = run_bass_kernel_spmd(nc3, maps3, core_ids=cores)
    out = np.stack([np.asarray(r3.results[b]["out"]) for b in cores], axis=0)
    return out
